# version 04
# baseline (speedup 1.0000x reference)
import numpy as np
import ml_dtypes
import concourse.bass as bass
import concourse.mybir as mybir
from concourse.bass_utils import run_bass_kernel_spmd

F32 = mybir.dt.float32
BF16 = mybir.dt.bfloat16
ALU = mybir.AluOpType
ACT = mybir.ActivationFunctionType
AX = mybir.AxisListType

NCORES = 8
D = 4096
DC = 32
TOK = 2048
NT = 16
MODC = 3072
HTOK = 1024
EPS = 1e-5
DEBUG = False


class Prog:

    def __init__(self):
        self.items = []
        self.ev = {}
        self.run = {}

    def add(self, eng, fn, waits=(), ev=None, sem=None, amount=1):
        if ev is not None:
            assert sem is not None
            v = self.run.get(sem, 0) + amount
            self.run[sem] = v
            assert ev not in self.ev, ev
            self.ev[ev] = (sem, v)
        self.items.append((eng, fn, tuple(waits), ev, sem, amount))

    def emit(self, eng_name, E, sems):
        for (e, fn, waits, evname, sem, amount) in self.items:
            if e != eng_name:
                continue
            for w in waits:
                assert w in self.ev, f"unknown event {w}"
                s, v = self.ev[w]
                E.wait_ge(sems[s], v)
            inst = fn(E)
            if evname is not None:
                assert inst is not None, evname
                inst.then_inc(sems[sem], amount)


def build():
    nc = bass.Bass("TRN2", target_bir_lowering=False, debug=False)
    dt = nc.dram_tensor
    x_in = dt("x_in", [TOK, D], F32, kind="ExternalInput").ap()
    wmod_sl = dt("wmod_sl", [D, MODC], F32, kind="ExternalInput").ap()
    wbT = dt("wbT", [D, D], F32, kind="ExternalInput").ap()
    wg9t = dt("wg9t", [D, 9], F32, kind="ExternalInput").ap()
    acatT = dt("acatT", [D, 128], F32, kind="ExternalInput").ap()
    bcat_in = dt("bcat_in", [128, D], F32, kind="ExternalInput").ap()
    c_in = dt("c_in", [D], F32, kind="ExternalInput").ap()
    bb_in = dt("bb_in", [D], F32, kind="ExternalInput").ap()
    bthr_in = dt("bthr_in", [1, 1], F32, kind="ExternalInput").ap()
    eye_in = dt("eye_in", [128, 128], F32, kind="ExternalInput").ap()
    e16_in = dt("e16_in", [8, 128], F32, kind="ExternalInput").ap()
    ones_in = dt("ones_in", [1, 128], F32, kind="ExternalInput").ap()
    y_out = dt("y_out", [TOK, D], F32, kind="ExternalOutput").ap()
    dbg = {}
    if DEBUG:
        dbg["d_modag"] = dt("d_modag", [NCORES * MODC], F32, kind="ExternalOutput").ap()
        dbg["d_cols"] = dt("d_cols", [8, 128, DC], F32, kind="ExternalOutput").ap()
        dbg["d_stats"] = dt("d_stats", [128, 2, 16], F32, kind="ExternalOutput").ap()
        dbg["d_u1"] = dt("d_u1", [128, D], BF16, kind="ExternalOutput").ap()
        dbg["d_x1"] = dt("d_x1", [128, D], F32, kind="ExternalOutput").ap()
        dbg["d_hw"] = dt("d_hw", [128, HTOK], BF16, kind="ExternalOutput").ap()
        dbg["d_g9"] = dt("d_g9", [128, 8, 9], F32, kind="ExternalOutput").ap()
        dbg["d_w8"] = dt("d_w8", [128, 8, 8], F32, kind="ExternalOutput").ap()
        dbg["d_wt"] = dt("d_wt", [8, HTOK], F32, kind="ExternalOutput").ap()
        dbg["d_qbar"] = dt("d_qbar", [128, 1], F32, kind="ExternalOutput").ap()
        dbg["d_matg"] = dt("d_matg", [128, D], F32, kind="ExternalOutput").ap()
        dbg["d_matbb"] = dt("d_matbb", [128, D], F32, kind="ExternalOutput").ap()
        dbg["d_modsl"] = dt("d_modsl", [MODC], F32, kind="ExternalOutput").ap()
        dbg["d_scolb"] = dt("d_scolb", [128, DC], BF16, kind="ExternalOutput").ap()
    modsl_d = dt("modsl_d", [MODC], F32).ap()
    modag_d = dt("modag_d", [NCORES * MODC], F32).ap()
    u1_d = dt("u1_d", [TOK, D], BF16).ap()
    x1_d = dt("x1_d", [TOK, D], F32).ap()
    rowd = dt("rowd", [4, D], F32).ap()

    from contextlib import ExitStack
    ctx = ExitStack()
    e = ctx.enter_context

    bigb = e(nc.sbuf_tensor("bigb", [128, 16, 2048], BF16, is_transpose_dest=True))
    matbuf = e(nc.sbuf_tensor("matbuf", [128, 2, D], F32))
    wring = e(nc.sbuf_tensor("wring", [128, 4, 512], BF16))
    acat_sb = e(nc.sbuf_tensor("acat_sb", [128, DC, 128], BF16))
    bcat_sb = e(nc.sbuf_tensor("bcat_sb", [128, D], BF16))
    wg9_sb = e(nc.sbuf_tensor("wg9_sb", [128, DC, 9], BF16))
    scol = e(nc.sbuf_tensor("scol", [128, DC], F32))
    scolb = e(nc.sbuf_tensor("scolb", [128, DC], BF16))
    bbasecol = e(nc.sbuf_tensor("bbasecol", [128, DC], F32))
    modcol = e(nc.sbuf_tensor("modcol", [128, 6, DC], F32))
    pcol = e(nc.sbuf_tensor("pcol", [128, DC], F32))
    qcol = e(nc.sbuf_tensor("qcol", [128, DC], F32))
    cb1col = e(nc.sbuf_tensor("cb1col", [128, DC], F32))
    gcol = e(nc.sbuf_tensor("gcol", [128, DC], F32))
    bbcol = e(nc.sbuf_tensor("bbcol", [128, DC], F32))
    tmpc = e(nc.sbuf_tensor("tmpc", [128, 4 * DC], F32))
    rows32a = e(nc.sbuf_tensor("rows32a", [32, 128], F32))
    rows32b = e(nc.sbuf_tensor("rows32b", [32, 128], F32))
    silu32 = e(nc.sbuf_tensor("silu32", [32, 128], F32))
    rows_sb = e(nc.sbuf_tensor("rows_sb", [1, 2, 512], F32))
    modrow = e(nc.sbuf_tensor("modrow", [1, 512], F32))
    eye_sb = e(nc.sbuf_tensor("eye_sb", [128, 128], F32))
    e16_sb = e(nc.sbuf_tensor("e16_sb", [8, 128], F32))
    ones_sb = e(nc.sbuf_tensor("ones_sb", [1, 128], F32))
    onescol = e(nc.sbuf_tensor("onescol", [128, 1], F32))
    bthr_sb = e(nc.sbuf_tensor("bthr_sb", [1, 1], F32))
    bthr_col = e(nc.sbuf_tensor("bthr_col", [128, 1], F32))
    qbar1 = e(nc.sbuf_tensor("qbar1", [1, 1], F32))
    qbar_col = e(nc.sbuf_tensor("qbar_col", [128, 1], F32))
    qs = e(nc.sbuf_tensor("qs", [128, 1], F32))
    epscol = e(nc.sbuf_tensor("epscol", [128, 1], F32))
    stats = e(nc.sbuf_tensor("stats", [128, 2, 16], F32))
    junkbuf = e(nc.sbuf_tensor("junkbuf", [128, D], F32))
    glog_sb = e(nc.sbuf_tensor("glog_sb", [9, 512], F32))
    g9 = e(nc.sbuf_tensor("g9", [128, 8, 9], F32))
    gst = e(nc.sbuf_tensor("gst", [128, 8, 8], F32))
    w8 = e(nc.sbuf_tensor("w8", [128, 8, 8], F32))
    wT_sb = e(nc.sbuf_tensor("wT_sb", [8, HTOK], F32))
    her_sb = e(nc.sbuf_tensor("her_sb", [128, 2, 512], F32))
    hw_sb = e(nc.sbuf_tensor("hw_sb", [128, HTOK], BF16))
    e1r = e(nc.sbuf_tensor("e1r", [128, 3, 512], F32))
    x1r = e(nc.sbuf_tensor("x1r", [128, 3, 512], F32))
    yr = e(nc.sbuf_tensor("yr", [128, 3, 512], F32))
    ps = e(nc.psum_tensor("ps", [128, 8, 512], F32))

    SEMS = ["ld", "xa", "gl", "wm", "mv", "md", "cc", "p0s", "p0v", "p0t",
            "ud1", "ud2", "va", "sa", "pa", "tr", "hb", "gt", "gs", "gv",
            "wl", "mm", "epv", "epp", "xr", "yd", "rw"]
    sems = {s: e(nc.semaphore(s)) for s in SEMS}
    P = Prog()

    def vchain(ops, waits, ev, sem):
        n = len(ops)
        for idx, fn in enumerate(ops):
            w = list(waits) if idx == 0 else [f"{ev}__c{idx - 1}"]
            evname = ev if idx == n - 1 else f"{ev}__c{idx}"
            P.add("v", fn, waits=w, ev=evname, sem=sem)

    def fchunks(c0, n):
        return bigb.ap()[:, c0:c0 + n, :].bitcast(F32).rearrange("p a b -> p (a b)")

    xbuf = [fchunks(0, 4), fchunks(4, 4)]
    u0buf = fchunks(8, 4)
    x1buf = fchunks(12, 4)
    u1buf = bigb.ap()[:, 8:10, :].rearrange("p a b -> p (a b)")
    hbT = bigb.ap().rearrange("p a (b t) -> p (a b) t", b=2)
    modring = matbuf.ap().rearrange("p a b -> p (a b)")
    Pm = matbuf.ap()[:, 0, :]
    Qm = matbuf.ap()[:, 1, :]
    Gm = matbuf.ap()[:, 0, :]
    BBm = matbuf.ap()[:, 1, :]

    def st(i, name):
        cols = {"s0": 0, "q0": 1, "m0": 2, "var": 3, "r0": 4, "nm0": 5,
                "dp": 6, "q1": 7, "m1": 8, "var1": 9, "r1": 10, "nm1": 11,
                "t0": 12, "t1": 13, "t2": 14}
        return stats.ap()[:, i % 2, cols[name]:cols[name] + 1]

    P.add("sy", lambda E: E.dma_start(out=eye_sb[:], in_=eye_in), ev="ld_eye", sem="ld", amount=16)
    P.add("sy", lambda E: E.dma_start(out=e16_sb[:], in_=e16_in), ev="ld_e16", sem="ld", amount=16)
    P.add("sy", lambda E: E.dma_start(out=ones_sb[:], in_=ones_in), ev="ld_ones", sem="ld", amount=16)
    P.add("sy", lambda E: E.dma_start(out=bthr_sb[:], in_=bthr_in), ev="ld_bthr", sem="ld", amount=16)
    P.add("sy", lambda E: E.dma_start(out=rows32a[:], in_=c_in.rearrange("(a k) -> a k", a=32)),
          ev="ld_c32", sem="ld", amount=16)
    P.add("sy", lambda E: E.dma_start(out=rows32b[:], in_=bb_in.rearrange("(a k) -> a k", a=32)),
          ev="ld_bb32", sem="ld", amount=16)
    gl_events = []
    for q in range(4):
        for tag, fn in ((f"ld_wg9_{q}", lambda E, q=q: E.dma_start(
                out=wg9_sb.ap()[:, 8 * q:8 * (q + 1), :],
                in_=wg9t[1024 * q:1024 * (q + 1), :].rearrange("(a p) m -> p a m", p=128))),
                        (f"ld_acat_{q}", lambda E, q=q: E.dma_start(
                out=acat_sb.ap()[:, 8 * q:8 * (q + 1), :],
                in_=acatT[1024 * q:1024 * (q + 1), :].rearrange("(a p) m -> p a m", p=128)))):
            waits = [gl_events[-2]] if len(gl_events) >= 2 else []
            P.add("g", fn, waits=waits, ev=tag, sem="gl", amount=16)
            gl_events.append(tag)
    P.add("g", lambda E: E.dma_start(out=bcat_sb[:], in_=bcat_in),
          waits=[gl_events[-2]], ev="ld_bcat", sem="gl", amount=16)

    P.add("sc", lambda E: E.activation(silu32[:], rows32a[:], ACT.Silu),
          waits=["ld_c32"], ev="silu", sem="p0s")
    P.add("t", lambda E: E.transpose(ps[:, 6, 0:32], silu32[:], eye_sb[0:32, 0:32]),
          waits=["silu", "ld_eye"], ev="tr_sc", sem="p0t")
    P.add("sc", lambda E: E.activation(scol[:], ps[:, 6, 0:32], ACT.Identity),
          waits=["tr_sc"], ev="ev_scol", sem="p0s")
    P.add("sc", lambda E: E.activation(scolb[:], ps[:, 6, 0:32], ACT.Identity),
          ev="ev_scolb", sem="p0s")
    P.add("t", lambda E: E.transpose(ps[:, 7, 0:32], rows32b[:], eye_sb[0:32, 0:32]),
          waits=["ld_bb32", "ld_eye"], ev="tr_bb", sem="p0t")
    P.add("sc", lambda E: E.activation(bbasecol[:], ps[:, 7, 0:32], ACT.Identity),
          waits=["tr_bb"], ev="ev_bb", sem="p0s")
    P.add("t", lambda E: E.matmul(ps[:, 7, 40:41], ones_sb[:], bthr_sb[:], start=True, stop=True),
          waits=["ld_ones", "ld_bthr", "ev_bb"], ev="mm_bthr", sem="p0t")
    P.add("sc", lambda E: E.activation(bthr_col[:], ps[:, 7, 40:41], ACT.Identity),
          waits=["mm_bthr"], ev="ev_bthr", sem="p0s")
    P.add("v", lambda E: E.memset(epscol[:], EPS), ev="ms_eps", sem="p0v")
    P.add("v", lambda E: E.memset(onescol[:], 1.0), ev="ms_ones", sem="p0v")

    for k in range(DC):
        ring = modring[:, 4096 * (k % 2):4096 * (k % 2) + 3072]
        waits = [f"mv{k - 2}"] if k >= 2 else ["ld_bcat"]
        P.add("g", lambda E, k=k, ring=ring: E.dma_start(
            out=ring, in_=wmod_sl[128 * k:128 * (k + 1), :]),
            waits=waits, ev=f"wm{k}", sem="wm", amount=16)

        def mv(E, k=k, ring=ring):
            inst = None
            for g in range(6):
                inst = E.matmul(ps[0:1, g, :], scol[:, k:k + 1], ring[:, 512 * g:512 * (g + 1)],
                                start=(k == 0), stop=(k == DC - 1), skip_group_check=True)
            return inst
        P.add("t", mv, waits=[f"wm{k}"] + (["ev_scol", "ev_scolb"] if k == 0 else []),
              ev=f"mv{k}", sem="mv")
    for g in range(6):
        P.add("sc", lambda E, g=g: E.activation(modrow[:], ps[0:1, g, :], ACT.Identity),
              waits=[f"mv{DC - 1}"] + ([f"md{g - 1}"] if g else []), ev=f"me{g}", sem="p0s")
        P.add("sc", lambda E, g=g: E.dma_start(
            out=modsl_d[512 * g:512 * (g + 1)].rearrange("(p n) -> p n", p=1), in_=modrow[:]),
            waits=[f"me{g}"], ev=f"md{g}", sem="md", amount=16)
    P.add("g", lambda E: E.collective_compute(
        "AllGather", ALU.bypass, replica_groups=[list(range(NCORES))],
        ins=[modsl_d], outs=[modag_d]),
        waits=["md5"], ev="cc", sem="cc")
    for grp in range(6):
        buf = rows32a if grp % 2 == 0 else rows32b
        waits = ["cc"]
        if grp >= 2:
            waits.append(f"ev_mc{grp - 2}")
        elif grp == 0:
            waits.append("ev_scolb")
        elif grp == 1:
            waits.append("ev_bb")
        P.add("sy", lambda E, grp=grp, buf=buf: E.dma_start(
            out=buf[:], in_=modag_d[4096 * grp:4096 * (grp + 1)].rearrange("(a k) -> a k", a=32)),
            waits=waits, ev=f"ld_mc{grp}", sem="ld", amount=16)
        P.add("t", lambda E, grp=grp, buf=buf: E.transpose(
            ps[:, 6 + grp % 2, 64:96], buf[:], eye_sb[0:32, 0:32]),
            waits=[f"ld_mc{grp}"] + ([f"ev_mc{grp - 2}"] if grp >= 2 else []) +
                  (["ev_scol", "ev_scolb"] if grp == 0 else []) + (["ev_bthr"] if grp == 1 else []),
            ev=f"tr_mc{grp}", sem="p0t")
        P.add("sc", lambda E, grp=grp: E.activation(
            modcol.ap()[:, grp, :], ps[:, 6 + grp % 2, 64:96], ACT.Identity),
            waits=[f"tr_mc{grp}"], ev=f"ev_mc{grp}", sem="p0s")

    SA_, CA_, GA_ = modcol.ap()[:, 0, :], modcol.ap()[:, 1, :], modcol.ap()[:, 2, :]
    SB_, CB_, GB_ = modcol.ap()[:, 3, :], modcol.ap()[:, 4, :], modcol.ap()[:, 5, :]
    T0 = tmpc.ap()[:, 0:DC]
    T1 = tmpc.ap()[:, DC:2 * DC]
    SGA = tmpc.ap()[:, 2 * DC:3 * DC]
    SGB = tmpc.ap()[:, 3 * DC:4 * DC]

    P.add("sc", lambda E: E.activation(SGA, GA_, ACT.Sign),
          waits=["ev_mc2"], ev="sgn_a", sem="p0s")
    P.add("sc", lambda E: E.activation(SGB, GB_, ACT.Sign),
          waits=["ev_mc5"], ev="sgn_b", sem="p0s")

    derive_ops = []
    for g_in, sgn, g_out in ((GA_, SGA, pcol.ap()[:]), (GB_, SGB, gcol.ap()[:])):
        derive_ops += [
            lambda E, g_in=g_in: E.tensor_single_scalar(T1, g_in, -1.0, ALU.mult),
            lambda E, g_in=g_in: E.tensor_tensor(T1, T1, g_in, ALU.max),
            lambda E: E.tensor_single_scalar(T1, T1, 0.25, ALU.max),
            lambda E, sgn=sgn, g_out=g_out: E.tensor_tensor(g_out, sgn, T1, ALU.mult),
        ]
    derive_ops += [
        lambda E: E.tensor_tensor(T0, pcol[:], SA_, ALU.mult),
        lambda E: E.tensor_single_scalar(qcol[:], T0, 2.0, ALU.mult),
        lambda E: E.tensor_scalar(T0, CA_, 1.0, 2.0, ALU.add, ALU.mult),
        lambda E: E.tensor_tensor(pcol[:], pcol[:], T0, ALU.mult),
        lambda E: E.tensor_single_scalar(cb1col[:], CB_, 1.0, ALU.add),
        lambda E: E.tensor_tensor(bbcol[:], gcol[:], bbasecol[:], ALU.mult),
        lambda E: E.tensor_reduce(qs[:], qcol[:], axis=AX.X, op=ALU.add),
    ]
    vchain(derive_ops, [f"ev_mc{g}" for g in range(6)] + ["ev_bb", "sgn_a", "sgn_b"],
           "derived", "p0v")

    P.add("t", lambda E: E.matmul(ps[0:1, 7, 48:49], qs[:], onescol[:], start=True, stop=True),
          waits=["derived", "ms_ones"], ev="mm_qb", sem="p0t")
    P.add("sc", lambda E: E.activation(qbar1[:], ps[0:1, 7, 48:49], ACT.Identity),
          waits=["mm_qb"], ev="ev_qb1", sem="p0s")
    P.add("v", lambda E: E.tensor_single_scalar(qbar1[:], qbar1[:], 1.0 / D, ALU.mult),
          waits=["ev_qb1"], ev="sc_qb", sem="p0v")
    P.add("t", lambda E: E.matmul(ps[:, 7, 56:57], ones_sb[:], qbar1[:], start=True, stop=True),
          waits=["sc_qb"], ev="mm_qbb", sem="p0t")
    P.add("sc", lambda E: E.activation(qbar_col[:], ps[:, 7, 56:57], ACT.Identity),
          waits=["mm_qbb"], ev="ev_qbar", sem="p0s")

    def build_mat(tag, col, slot, rowslot, extra_waits):
        P.add("t", lambda E: E.transpose(ps[0:32, 6, 0:128], col, eye_sb[:]),
              waits=extra_waits, ev=f"trm_{tag}", sem="p0t")
        P.add("sc", lambda E: E.activation(silu32[:], ps[0:32, 6, 0:128], ACT.Identity),
              waits=[f"trm_{tag}"], ev=f"evm_{tag}", sem="p0s")
        P.add("sc", lambda E: E.dma_start(out=rowd[rowslot, :].rearrange("(a k) -> a k", a=32),
                                          in_=silu32[:]),
              waits=[f"evm_{tag}"], ev=f"rw_{tag}", sem="rw", amount=16)
        for ch in range(8):
            rs = rows_sb.ap()[:, ch % 2, :]
            waits = [f"rw_{tag}"] if ch < 2 else [f"mmb_{tag}_{ch - 2}"]
            P.add("sy", lambda E, ch=ch, rs=rs: E.dma_start(
                out=rs, in_=rowd[rowslot, 512 * ch:512 * (ch + 1)].rearrange("(p n) -> p n", p=1)),
                waits=waits, ev=f"rl_{tag}_{ch}", sem="ld", amount=16)
            bank = 6 + ch % 2
            P.add("t", lambda E, ch=ch, rs=rs, bank=bank: E.matmul(
                ps[:, bank, :], ones_sb[:], rs, start=True, stop=True),
                waits=[f"rl_{tag}_{ch}"] + ([f"evb_{tag}_{ch - 2}"] if ch >= 2 else []),
                ev=f"mmb_{tag}_{ch}", sem="p0t")
            P.add("sc", lambda E, ch=ch, slot=slot, bank=bank: E.activation(
                matbuf.ap()[:, slot, 512 * ch:512 * (ch + 1)], ps[:, bank, :], ACT.Identity),
                waits=[f"mmb_{tag}_{ch}"], ev=f"evb_{tag}_{ch}", sem="p0s")

    build_mat("P", pcol[:], 0, 0, ["derived", "ev_qbar", f"mv{DC - 1}", "md5"])
    build_mat("Q", qcol[:], 1, 1, [f"evb_P_7", f"mv{DC - 1}", "md5"])

    NVA_PER_TILE = 4
    NSA_PER_TILE = 4
    for i in range(NT):
        xs = xbuf[i % 2]
        waits = []
        if i >= 2:
            waits.append(f"pa_x1_{i - 2}")
        P.add("sy", lambda E, i=i, xs=xs: E.dma_start(out=xs, in_=x_in[128 * i:128 * (i + 1), :]),
              waits=waits, ev=f"xa{i}", sem="xa", amount=16)
        P.add("v", lambda E, i=i, xs=xs: E.tensor_reduce(st(i, "s0"), xs, axis=AX.X, op=ALU.add),
              waits=[f"xa{i}"] + ([f"sa_u1_{i - 2}"] if i >= 2 else []),
              ev=f"pa_s0_{i}", sem="pa")
        P.add("sc", lambda E, i=i, xs=xs: E.activation(
            junkbuf[:], xs, ACT.Square, accum_out=st(i, "q0")),
            waits=[f"xa{i}"], ev=f"sa_q0_{i}", sem="sa")
        vchain([
            lambda E, i=i: E.tensor_single_scalar(st(i, "m0"), st(i, "s0"), 1.0 / D, ALU.mult),
            lambda E, i=i: E.tensor_tensor(st(i, "t0"), st(i, "m0"), st(i, "m0"), ALU.mult),
            lambda E, i=i: E.tensor_single_scalar(st(i, "var"), st(i, "q0"), 1.0 / D, ALU.mult),
            lambda E, i=i: E.tensor_tensor(st(i, "var"), st(i, "var"), st(i, "t0"), ALU.subtract),
        ], [f"pa_s0_{i}", f"sa_q0_{i}"] + (["ms_eps"] if i == 0 else []), f"va_s0_{i}", "va")
        P.add("sc", lambda E, i=i: E.activation(st(i, "t1"), st(i, "var"), ACT.Sqrt, bias=epscol[:]),
              waits=[f"va_s0_{i}"], ev=f"sa_sd0_{i}", sem="sa")
        vchain([
            lambda E, i=i: E.reciprocal(st(i, "r0"), st(i, "t1")),
            lambda E, i=i: E.scalar_tensor_tensor(st(i, "nm0"), st(i, "m0"), -1.0, st(i, "r0"),
                                                  ALU.mult, ALU.mult),
        ], [f"sa_sd0_{i}"], f"va_r0_{i}", "va")
        P.add("sc", lambda E, i=i, xs=xs: E.activation(
            u0buf, xs, ACT.Identity, bias=st(i, "nm0"), scale=st(i, "r0")),
            waits=[f"va_r0_{i}"] + ([f"ud1_{i - 1}"] if i else []),
            ev=f"sa_u0_{i}", sem="sa")
        P.add("v", lambda E, i=i: E.scalar_tensor_tensor(
            x1buf, u0buf, 1.0, Pm, ALU.mult, ALU.mult, accum_out=st(i, "dp")),
            waits=[f"sa_u0_{i}", "evb_P_7"] + ([f"ud2_{i - 1}"] if i else []),
            ev=f"va_t_{i}", sem="va")
        P.add("g", lambda E, i=i, xs=xs: E.tensor_tensor(x1buf, x1buf, xs, ALU.add),
              waits=[f"va_t_{i}", "evb_Q_7"], ev=f"pa_x1a_{i}", sem="pa")
        P.add("g", lambda E, i=i: E.tensor_tensor(x1buf, x1buf, Qm, ALU.add),
              ev=f"pa_x1_{i}", sem="pa")
        P.add("sc", lambda E, i=i: E.activation(
            junkbuf[:], x1buf, ACT.Square, accum_out=st(i, "q1")),
            waits=[f"pa_x1_{i}"], ev=f"sa_q1_{i}", sem="sa")
        vchain([
            lambda E, i=i: E.tensor_single_scalar(st(i, "m1"), st(i, "dp"), 1.0 / D, ALU.mult),
            lambda E, i=i: E.tensor_tensor(st(i, "m1"), st(i, "m1"), st(i, "m0"), ALU.add),
            lambda E, i=i: E.tensor_tensor(st(i, "m1"), st(i, "m1"), qbar_col[:], ALU.add),
            lambda E, i=i: E.tensor_tensor(st(i, "t0"), st(i, "m1"), st(i, "m1"), ALU.mult),
            lambda E, i=i: E.tensor_single_scalar(st(i, "var1"), st(i, "q1"), 1.0 / D, ALU.mult),
            lambda E, i=i: E.tensor_tensor(st(i, "var1"), st(i, "var1"), st(i, "t0"), ALU.subtract),
        ], [f"sa_q1_{i}", "ev_qbar"], f"va_s1_{i}", "va")
        P.add("sc", lambda E, i=i: E.activation(st(i, "t1"), st(i, "var1"), ACT.Sqrt, bias=epscol[:]),
              waits=[f"va_s1_{i}"], ev=f"sa_sd1_{i}", sem="sa")
        vchain([
            lambda E, i=i: E.reciprocal(st(i, "r1"), st(i, "t1")),
            lambda E, i=i: E.scalar_tensor_tensor(st(i, "nm1"), st(i, "m1"), -1.0, st(i, "r1"),
                                                  ALU.mult, ALU.mult),
        ], [f"sa_sd1_{i}"], f"va_r1_{i}", "va")
        P.add("sc", lambda E, i=i: E.activation(
            u1buf, x1buf, ACT.Identity, bias=st(i, "nm1"), scale=st(i, "r1")),
            waits=[f"va_r1_{i}"], ev=f"sa_u1_{i}", sem="sa")
        P.add("sc", lambda E, i=i: E.dma_start(out=u1_d[128 * i:128 * (i + 1), :], in_=u1buf),
              waits=[f"sa_u1_{i}"], ev=f"ud1_{i}", sem="ud1", amount=16)
        P.add("sc", lambda E, i=i: E.dma_start(out=x1_d[128 * i:128 * (i + 1), :], in_=x1buf),
              ev=f"ud2_{i}", sem="ud2", amount=16)

    build_mat("G", gcol[:], 0, 2, [f"va_t_{NT - 1}", f"pa_x1_{NT - 1}"])
    build_mat("BB", bbcol[:], 1, 3, ["evb_G_7"])

    phaseA_done = [f"ud1_{NT - 1}", f"ud2_{NT - 1}", f"sa_u1_{NT - 1}"]
    nmm = 0
    nep = 0
    for H in range(2):
        hh = f"h{H}"
        for dc in range(DC):
            for grp in range(2):
                waits = []
                ti = H * 64 + 2 * dc + grp
                if ti >= 4:
                    pi = ti - 4
                    pH, pr = divmod(pi, 64)
                    waits.append(f"tr_h{pH}_{pr // 2}_{pr % 2}")
                if dc == 0 and grp == 0:
                    waits += phaseA_done
                    if H == 1:
                        waits.append(f"mm_h0_oc7_moe7")
                P.add("sy", lambda E, H=H, dc=dc, grp=grp: E.dma_start_transpose(
                    out=hbT[:, dc, 512 * grp:512 * (grp + 1)],
                    in_=u1_d[HTOK * H + 512 * grp:HTOK * H + 512 * (grp + 1),
                             128 * dc:128 * (dc + 1)]),
                    waits=waits, ev=f"tr_{hh}_{dc}_{grp}", sem="tr", amount=16)
            P.add("sc", lambda E, dc=dc: E.activation(
                hbT[:, dc, :], hbT[:, dc, :], ACT.Identity,
                bias=modcol.ap()[:, 3, dc:dc + 1], scale=cb1col[:, dc:dc + 1]),
                waits=[f"tr_{hh}_{dc}_0", f"tr_{hh}_{dc}_1"] +
                      (["evb_BB_7"] if dc == 0 and H == 0 else []),
                ev=f"hb_{hh}_{dc}", sem="hb")
        hb_all = [f"hb_{hh}_{dc}" for dc in range(DC)]

        for n in range(2):
            gg = f"{hh}_{n}"
            def glog_mm(E, H=H, n=n):
                inst = None
                for dc in range(DC):
                    inst = E.matmul(ps[0:9, 0, :], wg9_sb.ap()[:, dc, :],
                                    hbT[:, dc, 512 * n:512 * (n + 1)],
                                    start=(dc == 0), stop=(dc == DC - 1),
                                    skip_group_check=True)
                return inst
            idx = 2 * H + n
            prev_gg = f"h{(idx - 1) // 2}_{(idx - 1) % 2}" if idx >= 1 else None
            waits = hb_all + [f"ld_wg9_{q}" for q in range(4)]
            if prev_gg:
                waits.append(f"gs_glog_{prev_gg}")
            if H == 1 and n == 0:
                waits.append("ep_e1_h0_7_0")
            P.add("t", glog_mm, waits=waits, ev=f"gt_glog_{gg}", sem="gt")
            P.add("sc", lambda E: E.activation(glog_sb[:], ps[0:9, 0, :], ACT.Identity),
                  waits=[f"gt_glog_{gg}"], ev=f"gs_glog_{gg}", sem="gs")
            def her_mm(E, H=H, n=n):
                inst = None
                for dc in range(DC):
                    inst = E.matmul(ps[:, 1, :], acat_sb.ap()[:, dc, :],
                                    hbT[:, dc, 512 * n:512 * (n + 1)],
                                    start=(dc == 0), stop=(dc == DC - 1),
                                    skip_group_check=True)
                return inst
            her_waits = [f"ld_acat_{q}" for q in range(4)]
            if prev_gg:
                her_waits.append(f"gs_her_{prev_gg}")
            if H == 1:
                her_waits.append(f"gv_hw_h0_{n}")
                if n == 0:
                    her_waits.append("ep_e1_h0_7_1")
            P.add("t", her_mm, waits=her_waits, ev=f"gt_her_{gg}", sem="gt")
            P.add("sc", lambda E, n=n: E.activation(her_sb.ap()[:, n, :], ps[:, 1, :], ACT.Identity),
                  waits=[f"gt_her_{gg}"], ev=f"gs_her_{gg}", sem="gs")
            for j in range(4):
                jj = 4 * n + j
                sl = 16 * (jj % 2)
                g9w = [f"gs_glog_{gg}"]
                if jj >= 2:
                    g9w.append(f"gs_g9_{hh}_{jj - 2}")
                elif H == 1:
                    g9w.append(f"gv_w_h0_{jj + 6}")
                    if jj == 0:
                        g9w.append("ep_e1_h0_7_2")
                P.add("t", lambda E, j=j, sl=sl: E.transpose(
                    ps[:, 2, sl:sl + 9], glog_sb[0:9, 128 * j:128 * (j + 1)], eye_sb[0:9, 0:9]),
                    waits=g9w, ev=f"gt_g9_{hh}_{jj}", sem="gt")
                P.add("sc", lambda E, jj=jj, sl=sl: E.activation(
                    g9.ap()[:, jj, :], ps[:, 2, sl:sl + 9], ACT.Identity),
                    waits=[f"gt_g9_{hh}_{jj}"], ev=f"gs_g9_{hh}_{jj}", sem="gs")
                def gate1(E, jj=jj):
                    return E.reduce_max(gst.ap()[:, jj, 0:1], g9.ap()[:, jj, 0:8],
                                        axis=AX.X, negate=True)
                P.add("v", gate1, waits=[f"gs_g9_{hh}_{jj}"], ev=f"gv_mx_{hh}_{jj}", sem="gv")
                P.add("sc", lambda E, jj=jj: E.activation(
                    g9.ap()[:, jj, 0:8], g9.ap()[:, jj, 0:8], ACT.Exp, bias=gst.ap()[:, jj, 0:1]),
                    waits=[f"gv_mx_{hh}_{jj}"], ev=f"gs_exp_{hh}_{jj}", sem="gs")
                P.add("sc", lambda E, jj=jj: E.activation(
                    gst.ap()[:, jj, 3:4], g9.ap()[:, jj, 8:9], ACT.Sigmoid, bias=bthr_col[:]),
                    ev=f"gs_sig_{hh}_{jj}", sem="gs")
                gv = gst.ap()[:, jj, :]
                w8j = w8.ap()[:, jj, :]
                g9j = g9.ap()[:, jj, 0:8]
                vchain([
                    lambda E, gv=gv, g9j=g9j: E.tensor_reduce(gv[:, 1:2], g9j, axis=AX.X, op=ALU.add),
                    lambda E, gv=gv: E.reciprocal(gv[:, 2:3], gv[:, 1:2]),
                    lambda E, gv=gv: E.tensor_single_scalar(gv[:, 3:4], gv[:, 3:4], -0.125, ALU.mult),
                    lambda E, gv=gv, g9j=g9j, w8j=w8j: E.tensor_scalar(
                        w8j, g9j, gv[:, 2:3], gv[:, 3:4], ALU.mult, ALU.add),
                    lambda E, w8j=w8j: E.tensor_single_scalar(w8j, w8j, 0.0, ALU.max),
                    lambda E, gv=gv, w8j=w8j: E.tensor_reduce(gv[:, 4:5], w8j, axis=AX.X, op=ALU.add),
                    lambda E, gv=gv: E.tensor_single_scalar(gv[:, 4:5], gv[:, 4:5], 1e-12, ALU.max),
                    lambda E, gv=gv: E.reciprocal(gv[:, 6:7], gv[:, 4:5]),
                    lambda E, gv=gv: E.tensor_single_scalar(gv[:, 6:7], gv[:, 6:7], 2.0, ALU.mult),
                    lambda E, gv=gv, w8j=w8j: E.tensor_scalar(w8j, w8j, gv[:, 6:7], None, ALU.mult),
                ], [f"gs_exp_{hh}_{jj}", f"gs_sig_{hh}_{jj}"], f"gv_w_{hh}_{jj}", "gv")
                P.add("t", lambda E, jj=jj: E.transpose(
                    ps[0:8, 2, 128 + 128 * (jj % 2):256 + 128 * (jj % 2)],
                    w8.ap()[:, jj, :], eye_sb[:]),
                    waits=[f"gv_w_{hh}_{jj}"] + ([f"gs_wT_{hh}_{jj - 2}"] if jj >= 2 else
                                                 ([f"gs_wT_h0_{jj + 6}"] if H == 1 else [])),
                    ev=f"gt_wT_{hh}_{jj}", sem="gt")
                P.add("sc", lambda E, jj=jj: E.activation(
                    wT_sb[0:8, 128 * jj:128 * (jj + 1)],
                    ps[0:8, 2, 128 + 128 * (jj % 2):256 + 128 * (jj % 2)], ACT.Identity),
                    waits=[f"gt_wT_{hh}_{jj}"], ev=f"gs_wT_{hh}_{jj}", sem="gs")
            wexp_waits = [f"gs_wT_{hh}_{4 * n + j}" for j in range(4)] + ["ld_e16"]
            if prev_gg:
                wexp_waits.append(f"gv_hw_{prev_gg}")
            if H == 1 and n == 0:
                wexp_waits.append("ep_e1_h0_7_3")
            P.add("t", lambda E, n=n: E.matmul(
                ps[:, 3, :], e16_sb[:], wT_sb[0:8, 512 * n:512 * (n + 1)],
                start=True, stop=True),
                waits=wexp_waits, ev=f"gt_wexp_{gg}", sem="gt")
            P.add("v", lambda E, n=n: E.scalar_tensor_tensor(
                hw_sb.ap()[:, 512 * n:512 * (n + 1)], her_sb.ap()[:, n, :], 1.0,
                ps[:, 3, :], ALU.mult, ALU.mult),
                waits=[f"gt_wexp_{gg}", f"gs_her_{gg}"], ev=f"gv_hw_{gg}", sem="gv")

        for oc in range(8):
            for dc in range(DC):
                gidx = (H * 8 + oc) * DC + dc
                slot = gidx % 4
                waits = []
                if gidx >= 4:
                    pd, pc_ = divmod(gidx - 4, DC)
                    waits.append(f"mm_h{pd // 8}_oc{pd % 8}_dc{pc_}")
                P.add("g", lambda E, H=H, oc=oc, dc=dc, slot=slot: E.dma_start(
                    out=wring.ap()[:, slot, :],
                    in_=wbT[128 * dc:128 * (dc + 1), 512 * oc:512 * (oc + 1)]),
                    waits=waits, ev=f"wl_{hh}_{oc}_{dc}", sem="wl", amount=16)
            for dc in range(DC):
                gidx = (H * 8 + oc) * DC + dc
                slot = gidx % 4
                def base_mm(E, dc=dc, slot=slot):
                    inst = None
                    for j in range(8):
                        inst = E.matmul(ps[:, j, :], hbT[:, dc, 128 * j:128 * (j + 1)],
                                        wring.ap()[:, slot, :], start=(dc == 0), stop=False,
                                        skip_group_check=True)
                    return inst
                waits = [f"wl_{hh}_{oc}_{dc}"]
                if dc == 0:
                    waits += [f"gv_hw_{hh}_0", f"gv_hw_{hh}_1"]
                    if oc > 0:
                        waits += [f"ep_e1_{hh}_{oc - 1}_{j}" for j in range(8)]
                    elif H == 1:
                        waits += [f"ep_e1_h0_7_{j}" for j in range(8)]
                    else:
                        waits += ["evb_G_7", "evb_BB_7"]
                P.add("t", base_mm, waits=waits, ev=f"mm_{hh}_oc{oc}_dc{dc}", sem="mm")
            for j in range(8):
                P.add("t", lambda E, H=H, oc=oc, j=j: E.matmul(
                    ps[:, j, :], hw_sb.ap()[:, 128 * j:128 * (j + 1)],
                    bcat_sb.ap()[:, 512 * oc:512 * (oc + 1)],
                    start=False, stop=True, skip_group_check=True),
                    waits=([f"mm_{hh}_oc{oc}_dc{DC - 1}", "ld_bcat"] if j == 0 else []),
                    ev=f"mm_{hh}_oc{oc}_moe{j}", sem="mm")
                k = (H * 8 + oc) * 8 + j
                r3 = k % 3
                P.add("sc", lambda E, H=H, oc=oc, j=j, r3=r3: E.dma_start(
                    out=x1r.ap()[:, r3, :],
                    in_=x1_d[HTOK * H + 128 * j:HTOK * H + 128 * (j + 1),
                             512 * oc:512 * (oc + 1)]),
                    waits=([f"ep_y_h{(k - 3) // 64}_{((k - 3) // 8) % 8}_{(k - 3) % 8}"] if k >= 3 else []),
                    ev=f"xr_{hh}_{oc}_{j}", sem="xr", amount=16)
                P.add("v", lambda E, oc=oc, j=j, r3=r3, H=H: E.scalar_tensor_tensor(
                    e1r.ap()[:, r3, :], Gm[:, 512 * oc:512 * (oc + 1)],
                    1.0, ps[:, j, :], ALU.mult, ALU.mult),
                    waits=[f"mm_{hh}_oc{oc}_moe{j}"] +
                          ([f"ep_p_h{(k - 3) // 64}_{((k - 3) // 8) % 8}_{(k - 3) % 8}"] if k >= 3 else []) +
                          (["evb_G_7", "evb_BB_7"] if k == 0 else []),
                    ev=f"ep_e1_{hh}_{oc}_{j}", sem="epv")
                P.add("g", lambda E, oc=oc, r3=r3, H=H: E.tensor_tensor(
                    e1r.ap()[:, r3, :], e1r.ap()[:, r3, :],
                    BBm[:, 512 * oc:512 * (oc + 1)], ALU.add),
                    waits=[f"ep_e1_{hh}_{oc}_{j}"], ev=f"ep_p_{hh}_{oc}_{j}", sem="epp")
                P.add("v", lambda E, r3=r3: E.tensor_tensor(
                    yr.ap()[:, r3, :], e1r.ap()[:, r3, :], x1r.ap()[:, r3, :], ALU.add),
                    waits=[f"ep_p_{hh}_{oc}_{j}", f"xr_{hh}_{oc}_{j}"] +
                          ([f"yd_h{(k - 3) // 64}_{((k - 3) // 8) % 8}_{(k - 3) % 8}"] if k >= 3 else []),
                    ev=f"ep_y_{hh}_{oc}_{j}", sem="epv")
                P.add("sy", lambda E, H=H, oc=oc, j=j, r3=r3: E.dma_start(
                    out=y_out[HTOK * H + 128 * j:HTOK * H + 128 * (j + 1),
                              512 * oc:512 * (oc + 1)],
                    in_=yr.ap()[:, r3, :]),
                    waits=[f"ep_y_{hh}_{oc}_{j}"], ev=f"yd_{hh}_{oc}_{j}", sem="yd", amount=16)

    if DEBUG:
        all_ep = ["ep_y_h1_7_7"]
        cols_list = [scol, bbasecol, pcol, qcol, cb1col, gcol, bbcol, scol]
        P.add("sy", lambda E: E.dma_start(out=dbg["d_modag"].rearrange("(p n) -> p n", p=1),
                                          in_=modag_d.rearrange("(p n) -> p n", p=1)),
              waits=all_ep, ev="dbg0", sem="yd", amount=16)
        for ci, cb in enumerate(cols_list):
            P.add("sy", lambda E, ci=ci, cb=cb: E.dma_start(
                out=dbg["d_cols"][ci], in_=cb[:]),
                ev=f"dbgc{ci}", sem="yd", amount=16)
        P.add("sy", lambda E: E.dma_start(out=dbg["d_stats"], in_=stats[:]), ev="dbg1", sem="yd", amount=16)
        P.add("sy", lambda E: E.dma_start(out=dbg["d_u1"], in_=u1_d[0:128, :]), ev="dbg2", sem="yd", amount=16)
        P.add("sy", lambda E: E.dma_start(out=dbg["d_x1"], in_=x1_d[0:128, :]), ev="dbg3", sem="yd", amount=16)
        P.add("sy", lambda E: E.dma_start(out=dbg["d_hw"], in_=hw_sb[:]), ev="dbg4", sem="yd", amount=16)
        P.add("sy", lambda E: E.dma_start(out=dbg["d_g9"], in_=g9[:]), ev="dbg5", sem="yd", amount=16)
        P.add("sy", lambda E: E.dma_start(out=dbg["d_w8"], in_=w8[:]), ev="dbg6", sem="yd", amount=16)
        P.add("sy", lambda E: E.dma_start(out=dbg["d_wt"], in_=wT_sb[:]), ev="dbg7", sem="yd", amount=16)
        P.add("sy", lambda E: E.dma_start(out=dbg["d_qbar"], in_=qbar_col[:]), ev="dbg8", sem="yd", amount=16)
        P.add("sy", lambda E: E.dma_start(out=dbg["d_matg"], in_=Gm), ev="dbg9", sem="yd", amount=16)
        P.add("sy", lambda E: E.dma_start(out=dbg["d_matbb"], in_=BBm), ev="dbg10", sem="yd", amount=16)
        P.add("sy", lambda E: E.dma_start(out=dbg["d_modsl"].rearrange("(p n) -> p n", p=1),
                                          in_=modsl_d.rearrange("(p n) -> p n", p=1)),
              waits=["md5"], ev="dbg11", sem="yd", amount=16)
        P.add("sy", lambda E: E.dma_start(out=dbg["d_scolb"], in_=scolb[:]),
              waits=["ev_scolb"], ev="dbg12", sem="yd", amount=16)

    def final_wait(E):
        return E.engine_nop()
    fw = [f"yd_h{H}_{oc}_{j}" for H in range(2) for oc in range(8) for j in range(8)]
    if DEBUG:
        fw += ["dbg0", "dbg1", "dbg2", "dbg3", "dbg4", "dbg5", "dbg6", "dbg7",
               "dbg8", "dbg9", "dbg10", "dbg11", "dbg12"] + [f"dbgc{ci}" for ci in range(8)]
    P.add("v", final_wait, waits=fw)

    with nc.Block() as blk:
        @blk.sync
        def _(E):
            P.emit("sy", E, sems)

        @blk.gpsimd
        def _(E):
            P.emit("g", E, sems)

        @blk.scalar
        def _(E):
            P.emit("sc", E, sems)

        @blk.vector
        def _(E):
            P.emit("v", E, sems)

        @blk.tensor
        def _(E):
            P.emit("t", E, sems)

    ctx.close()
    return nc


_NC_CACHE = None


def kernel(L0_attn_x, L0_ffn_W_base, L0_ffn_b_base, L0_ffn_W_gate, L0_ffn_W_thr,
           L0_ffn_b_thr, L0_ffn_A_w, L0_ffn_B_w, L0_c, L0_wmod):
    global _NC_CACHE
    x = np.ascontiguousarray(np.asarray(L0_attn_x, dtype=np.float32).reshape(16384, D))
    wbT = np.ascontiguousarray(np.asarray(L0_ffn_W_base, dtype=np.float32).T)
    wg9t = np.ascontiguousarray(np.concatenate(
        [np.asarray(L0_ffn_W_gate, dtype=np.float32),
         np.asarray(L0_ffn_W_thr, dtype=np.float32)[None, :]], axis=0).T)
    acatT = np.ascontiguousarray(np.asarray(L0_ffn_A_w, dtype=np.float32).reshape(128, D).T)
    bcat = np.ascontiguousarray(np.asarray(L0_ffn_B_w, dtype=np.float32)
                                .transpose(0, 2, 1).reshape(128, D))
    c = np.ascontiguousarray(np.asarray(L0_c, dtype=np.float32).reshape(D))
    bb = np.ascontiguousarray(np.asarray(L0_ffn_b_base, dtype=np.float32).reshape(D))
    bthr = np.ascontiguousarray(np.asarray(L0_ffn_b_thr, dtype=np.float32).reshape(1, 1))
    wmod = np.asarray(L0_wmod, dtype=np.float32)
    eye = np.eye(128, dtype=np.float32)
    e16 = np.zeros((8, 128), dtype=np.float32)
    for ee in range(8):
        e16[ee, 16 * ee:16 * (ee + 1)] = 1.0
    ones = np.ones((1, 128), dtype=np.float32)

    if _NC_CACHE is None:
        _NC_CACHE = build()
    nc = _NC_CACHE

    in_maps = []
    for r in range(NCORES):
        in_maps.append(dict(
            x_in=np.ascontiguousarray(x[TOK * r:TOK * (r + 1)]),
            wmod_sl=np.ascontiguousarray(wmod[:, MODC * r:MODC * (r + 1)]),
            wbT=wbT, wg9t=wg9t, acatT=acatT, bcat_in=bcat,
            c_in=c, bb_in=bb, bthr_in=bthr,
            eye_in=eye, e16_in=e16, ones_in=ones,
        ))
    res = run_bass_kernel_spmd(nc, in_maps, core_ids=list(range(NCORES)))
    y = np.concatenate([np.asarray(res.results[r]["y_out"]) for r in range(NCORES)], axis=0)
    return y.reshape(1, 16384, D).astype(np.float32)


# version 06
# speedup vs baseline: 1.0059x; 1.0059x over previous
import numpy as np
import ml_dtypes
import concourse.bass as bass
import concourse.mybir as mybir
from concourse.bass_utils import run_bass_kernel_spmd

F32 = mybir.dt.float32
BF16 = mybir.dt.bfloat16
ALU = mybir.AluOpType
ACT = mybir.ActivationFunctionType
AX = mybir.AxisListType

NCORES = 8
D = 4096
DC = 32
TOK = 2048
NT = 16
MODC = 3072
HTOK = 1024
EPS = 1e-5
DEBUG = False


class Prog:

    def __init__(self):
        self.items = []
        self.ev = {}
        self.run = {}

    def add(self, eng, fn, waits=(), ev=None, sem=None, amount=1):
        if ev is not None:
            assert sem is not None
            v = self.run.get(sem, 0) + amount
            self.run[sem] = v
            assert ev not in self.ev, ev
            self.ev[ev] = (sem, v)
        self.items.append((eng, fn, tuple(waits), ev, sem, amount))

    def emit(self, eng_name, E, sems):
        for (e, fn, waits, evname, sem, amount) in self.items:
            if e != eng_name:
                continue
            for w in waits:
                assert w in self.ev, f"unknown event {w}"
                s, v = self.ev[w]
                E.wait_ge(sems[s], v)
            inst = fn(E)
            if evname is not None:
                assert inst is not None, evname
                inst.then_inc(sems[sem], amount)


def build():
    nc = bass.Bass("TRN2", target_bir_lowering=False, debug=False)
    dt = nc.dram_tensor
    x_in = dt("x_in", [TOK, D], F32, kind="ExternalInput").ap()
    wmod_sl = dt("wmod_sl", [D, MODC], F32, kind="ExternalInput").ap()
    wbT = dt("wbT", [D, D], F32, kind="ExternalInput").ap()
    wg9t = dt("wg9t", [D, 9], F32, kind="ExternalInput").ap()
    acatT = dt("acatT", [D, 128], F32, kind="ExternalInput").ap()
    bcat_in = dt("bcat_in", [128, D], F32, kind="ExternalInput").ap()
    c_in = dt("c_in", [D], F32, kind="ExternalInput").ap()
    bb_in = dt("bb_in", [D], F32, kind="ExternalInput").ap()
    bthr_in = dt("bthr_in", [1, 1], F32, kind="ExternalInput").ap()
    eye_in = dt("eye_in", [128, 128], F32, kind="ExternalInput").ap()
    e16_in = dt("e16_in", [8, 128], F32, kind="ExternalInput").ap()
    ones_in = dt("ones_in", [1, 128], F32, kind="ExternalInput").ap()
    y_out = dt("y_out", [TOK, D], F32, kind="ExternalOutput").ap()
    dbg = {}
    if DEBUG:
        dbg["d_modag"] = dt("d_modag", [NCORES * MODC], F32, kind="ExternalOutput").ap()
        dbg["d_cols"] = dt("d_cols", [8, 128, DC], F32, kind="ExternalOutput").ap()
        dbg["d_stats"] = dt("d_stats", [128, 2, 16], F32, kind="ExternalOutput").ap()
        dbg["d_u1"] = dt("d_u1", [128, D], BF16, kind="ExternalOutput").ap()
        dbg["d_x1"] = dt("d_x1", [128, D], F32, kind="ExternalOutput").ap()
        dbg["d_hw"] = dt("d_hw", [128, HTOK], BF16, kind="ExternalOutput").ap()
        dbg["d_g9"] = dt("d_g9", [128, 8, 9], F32, kind="ExternalOutput").ap()
        dbg["d_w8"] = dt("d_w8", [128, 8, 8], F32, kind="ExternalOutput").ap()
        dbg["d_wt"] = dt("d_wt", [8, HTOK], F32, kind="ExternalOutput").ap()
        dbg["d_qbar"] = dt("d_qbar", [128, 1], F32, kind="ExternalOutput").ap()
        dbg["d_matg"] = dt("d_matg", [128, D], F32, kind="ExternalOutput").ap()
        dbg["d_matbb"] = dt("d_matbb", [128, D], F32, kind="ExternalOutput").ap()
        dbg["d_modsl"] = dt("d_modsl", [MODC], F32, kind="ExternalOutput").ap()
        dbg["d_scolb"] = dt("d_scolb", [128, DC], BF16, kind="ExternalOutput").ap()
    modsl_d = dt("modsl_d", [MODC], F32).ap()
    modag_d = dt("modag_d", [NCORES * MODC], F32).ap()
    u1_d = dt("u1_d", [TOK, D], BF16).ap()
    x1_d = dt("x1_d", [TOK, D], F32).ap()
    rowd = dt("rowd", [4, D], F32).ap()

    from contextlib import ExitStack
    ctx = ExitStack()
    e = ctx.enter_context

    bigb = e(nc.sbuf_tensor("bigb", [128, 16, 2048], BF16, is_transpose_dest=True))
    matbuf = e(nc.sbuf_tensor("matbuf", [128, 2, D], F32))
    wring = e(nc.sbuf_tensor("wring", [128, 4, 512], BF16))
    acat_sb = e(nc.sbuf_tensor("acat_sb", [128, DC, 128], BF16))
    bcat_sb = e(nc.sbuf_tensor("bcat_sb", [128, D], BF16))
    wg9_sb = e(nc.sbuf_tensor("wg9_sb", [128, DC, 9], BF16))
    scol = e(nc.sbuf_tensor("scol", [128, DC], F32))
    scolb = e(nc.sbuf_tensor("scolb", [128, DC], BF16))
    bbasecol = e(nc.sbuf_tensor("bbasecol", [128, DC], F32))
    modcol = e(nc.sbuf_tensor("modcol", [128, 6, DC], F32))
    pcol = e(nc.sbuf_tensor("pcol", [128, DC], F32))
    qcol = e(nc.sbuf_tensor("qcol", [128, DC], F32))
    cb1col = e(nc.sbuf_tensor("cb1col", [128, DC], F32))
    gcol = e(nc.sbuf_tensor("gcol", [128, DC], F32))
    bbcol = e(nc.sbuf_tensor("bbcol", [128, DC], F32))
    tmpc = e(nc.sbuf_tensor("tmpc", [128, 4 * DC], F32))
    rows32a = e(nc.sbuf_tensor("rows32a", [32, 128], F32))
    rows32b = e(nc.sbuf_tensor("rows32b", [32, 128], F32))
    silu32 = e(nc.sbuf_tensor("silu32", [32, 128], F32))
    rows_sb = e(nc.sbuf_tensor("rows_sb", [1, 2, 512], F32))
    modrow = e(nc.sbuf_tensor("modrow", [1, 512], F32))
    eye_sb = e(nc.sbuf_tensor("eye_sb", [128, 128], F32))
    e16_sb = e(nc.sbuf_tensor("e16_sb", [8, 128], F32))
    ones_sb = e(nc.sbuf_tensor("ones_sb", [1, 128], F32))
    onescol = e(nc.sbuf_tensor("onescol", [128, 1], F32))
    bthr_sb = e(nc.sbuf_tensor("bthr_sb", [1, 1], F32))
    bthr_col = e(nc.sbuf_tensor("bthr_col", [128, 1], F32))
    qbar1 = e(nc.sbuf_tensor("qbar1", [1, 1], F32))
    qbar_col = e(nc.sbuf_tensor("qbar_col", [128, 1], F32))
    qs = e(nc.sbuf_tensor("qs", [128, 1], F32))
    epscol = e(nc.sbuf_tensor("epscol", [128, 1], F32))
    stats = e(nc.sbuf_tensor("stats", [128, 2, 16], F32))
    junkbuf = e(nc.sbuf_tensor("junkbuf", [128, D], F32))
    glog_sb = e(nc.sbuf_tensor("glog_sb", [9, 512], F32))
    g9 = e(nc.sbuf_tensor("g9", [128, 8, 9], F32))
    gst = e(nc.sbuf_tensor("gst", [128, 8, 8], F32))
    w8 = e(nc.sbuf_tensor("w8", [128, 8, 8], F32))
    wT_sb = e(nc.sbuf_tensor("wT_sb", [8, HTOK], F32))
    her_sb = e(nc.sbuf_tensor("her_sb", [128, 2, 512], F32))
    hw_sb = e(nc.sbuf_tensor("hw_sb", [128, HTOK], BF16))
    e1r = e(nc.sbuf_tensor("e1r", [128, 3, 512], F32))
    x1r = e(nc.sbuf_tensor("x1r", [128, 3, 512], F32))
    yr = e(nc.sbuf_tensor("yr", [128, 3, 512], F32))
    ps = e(nc.psum_tensor("ps", [128, 8, 512], F32))

    SEMS = ["ld", "xa", "gl", "wm", "mv", "md", "cc", "p0s", "p0v", "p0t",
            "ud1", "ud2", "va", "sa", "pa", "tr", "hb", "gt", "gs", "gv",
            "wl", "mm", "epv", "epp", "xr", "yd", "rw"]
    sems = {s: e(nc.semaphore(s)) for s in SEMS}
    P = Prog()

    def vchain(ops, waits, ev, sem):
        n = len(ops)
        for idx, fn in enumerate(ops):
            w = list(waits) if idx == 0 else [f"{ev}__c{idx - 1}"]
            evname = ev if idx == n - 1 else f"{ev}__c{idx}"
            P.add("v", fn, waits=w, ev=evname, sem=sem)

    def fchunks(c0, n):
        return bigb.ap()[:, c0:c0 + n, :].bitcast(F32).rearrange("p a b -> p (a b)")

    xbuf = [fchunks(0, 4), fchunks(4, 4)]
    u0buf = fchunks(8, 4)
    x1buf = fchunks(12, 4)
    u1buf = bigb.ap()[:, 8:10, :].rearrange("p a b -> p (a b)")
    hbT = bigb.ap().rearrange("p a (b t) -> p (a b) t", b=2)
    modring = matbuf.ap().rearrange("p a b -> p (a b)")
    Pm = matbuf.ap()[:, 0, :]
    Qm = matbuf.ap()[:, 1, :]
    Gm = matbuf.ap()[:, 0, :]
    BBm = matbuf.ap()[:, 1, :]

    def st(i, name):
        cols = {"s0": 0, "q0": 1, "m0": 2, "var": 3, "r0": 4, "nm0": 5,
                "dp": 6, "q1": 7, "m1": 8, "var1": 9, "r1": 10, "nm1": 11,
                "t0": 12, "t1": 13, "t2": 14}
        return stats.ap()[:, i % 2, cols[name]:cols[name] + 1]

    P.add("sy", lambda E: E.dma_start(out=eye_sb[:], in_=eye_in), ev="ld_eye", sem="ld", amount=16)
    P.add("sy", lambda E: E.dma_start(out=e16_sb[:], in_=e16_in), ev="ld_e16", sem="ld", amount=16)
    P.add("sy", lambda E: E.dma_start(out=ones_sb[:], in_=ones_in), ev="ld_ones", sem="ld", amount=16)
    P.add("sy", lambda E: E.dma_start(out=bthr_sb[:], in_=bthr_in), ev="ld_bthr", sem="ld", amount=16)
    P.add("sy", lambda E: E.dma_start(out=rows32a[:], in_=c_in.rearrange("(a k) -> a k", a=32)),
          ev="ld_c32", sem="ld", amount=16)
    P.add("sy", lambda E: E.dma_start(out=rows32b[:], in_=bb_in.rearrange("(a k) -> a k", a=32)),
          ev="ld_bb32", sem="ld", amount=16)
    gl_events = []
    for q in range(4):
        for tag, fn in ((f"ld_wg9_{q}", lambda E, q=q: E.dma_start(
                out=wg9_sb.ap()[:, 8 * q:8 * (q + 1), :],
                in_=wg9t[1024 * q:1024 * (q + 1), :].rearrange("(a p) m -> p a m", p=128))),
                        (f"ld_acat_{q}", lambda E, q=q: E.dma_start(
                out=acat_sb.ap()[:, 8 * q:8 * (q + 1), :],
                in_=acatT[1024 * q:1024 * (q + 1), :].rearrange("(a p) m -> p a m", p=128)))):
            waits = [gl_events[-2]] if len(gl_events) >= 2 else []
            P.add("g", fn, waits=waits, ev=tag, sem="gl", amount=16)
            gl_events.append(tag)
    P.add("g", lambda E: E.dma_start(out=bcat_sb[:], in_=bcat_in),
          waits=[gl_events[-2]], ev="ld_bcat", sem="gl", amount=16)

    P.add("sc", lambda E: E.activation(silu32[:], rows32a[:], ACT.Silu),
          waits=["ld_c32"], ev="silu", sem="p0s")
    P.add("t", lambda E: E.transpose(ps[:, 6, 0:32], silu32[:], eye_sb[0:32, 0:32]),
          waits=["silu", "ld_eye"], ev="tr_sc", sem="p0t")
    P.add("sc", lambda E: E.activation(scol[:], ps[:, 6, 0:32], ACT.Identity),
          waits=["tr_sc"], ev="ev_scol", sem="p0s")
    P.add("sc", lambda E: E.activation(scolb[:], ps[:, 6, 0:32], ACT.Identity),
          ev="ev_scolb", sem="p0s")
    P.add("t", lambda E: E.transpose(ps[:, 7, 0:32], rows32b[:], eye_sb[0:32, 0:32]),
          waits=["ld_bb32", "ld_eye"], ev="tr_bb", sem="p0t")
    P.add("sc", lambda E: E.activation(bbasecol[:], ps[:, 7, 0:32], ACT.Identity),
          waits=["tr_bb"], ev="ev_bb", sem="p0s")
    P.add("t", lambda E: E.matmul(ps[:, 7, 40:41], ones_sb[:], bthr_sb[:], start=True, stop=True),
          waits=["ld_ones", "ld_bthr", "ev_bb"], ev="mm_bthr", sem="p0t")
    P.add("sc", lambda E: E.activation(bthr_col[:], ps[:, 7, 40:41], ACT.Identity),
          waits=["mm_bthr"], ev="ev_bthr", sem="p0s")
    P.add("v", lambda E: E.memset(epscol[:], EPS), ev="ms_eps", sem="p0v")
    P.add("v", lambda E: E.memset(onescol[:], 1.0), ev="ms_ones", sem="p0v")

    for k in range(DC):
        ring = modring[:, 4096 * (k % 2):4096 * (k % 2) + 3072]
        waits = [f"mv{k - 2}"] if k >= 2 else ["ld_bcat"]
        P.add("g", lambda E, k=k, ring=ring: E.dma_start(
            out=ring, in_=wmod_sl[128 * k:128 * (k + 1), :]),
            waits=waits, ev=f"wm{k}", sem="wm", amount=16)

        def mv(E, k=k, ring=ring):
            inst = None
            for g in range(6):
                inst = E.matmul(ps[0:1, g, :], scol[:, k:k + 1], ring[:, 512 * g:512 * (g + 1)],
                                start=(k == 0), stop=(k == DC - 1), skip_group_check=True)
            return inst
        P.add("t", mv, waits=[f"wm{k}"] + (["ev_scol", "ev_scolb"] if k == 0 else []),
              ev=f"mv{k}", sem="mv")
    for g in range(6):
        P.add("sc", lambda E, g=g: E.activation(modrow[:], ps[0:1, g, :], ACT.Identity),
              waits=[f"mv{DC - 1}"] + ([f"md{g - 1}"] if g else []), ev=f"me{g}", sem="p0s")
        P.add("sc", lambda E, g=g: E.dma_start(
            out=modsl_d[512 * g:512 * (g + 1)].rearrange("(p n) -> p n", p=1), in_=modrow[:]),
            waits=[f"me{g}"], ev=f"md{g}", sem="md", amount=16)
    P.add("g", lambda E: E.collective_compute(
        "AllGather", ALU.bypass, replica_groups=[list(range(NCORES))],
        ins=[modsl_d], outs=[modag_d]),
        waits=["md5"], ev="cc", sem="cc")
    for grp in range(6):
        buf = rows32a if grp % 2 == 0 else rows32b
        waits = ["cc"]
        if grp >= 2:
            waits.append(f"ev_mc{grp - 2}")
        elif grp == 0:
            waits.append("ev_scolb")
        elif grp == 1:
            waits.append("ev_bb")
        P.add("sy", lambda E, grp=grp, buf=buf: E.dma_start(
            out=buf[:], in_=modag_d[4096 * grp:4096 * (grp + 1)].rearrange("(a k) -> a k", a=32)),
            waits=waits, ev=f"ld_mc{grp}", sem="ld", amount=16)
        P.add("t", lambda E, grp=grp, buf=buf: E.transpose(
            ps[:, 6 + grp % 2, 64:96], buf[:], eye_sb[0:32, 0:32]),
            waits=[f"ld_mc{grp}"] + ([f"ev_mc{grp - 2}"] if grp >= 2 else []) +
                  (["ev_scol", "ev_scolb"] if grp == 0 else []) + (["ev_bthr"] if grp == 1 else []),
            ev=f"tr_mc{grp}", sem="p0t")
        P.add("sc", lambda E, grp=grp: E.activation(
            modcol.ap()[:, grp, :], ps[:, 6 + grp % 2, 64:96], ACT.Identity),
            waits=[f"tr_mc{grp}"], ev=f"ev_mc{grp}", sem="p0s")

    SA_, CA_, GA_ = modcol.ap()[:, 0, :], modcol.ap()[:, 1, :], modcol.ap()[:, 2, :]
    SB_, CB_, GB_ = modcol.ap()[:, 3, :], modcol.ap()[:, 4, :], modcol.ap()[:, 5, :]
    T0 = tmpc.ap()[:, 0:DC]
    T1 = tmpc.ap()[:, DC:2 * DC]
    SGA = tmpc.ap()[:, 2 * DC:3 * DC]
    SGB = tmpc.ap()[:, 3 * DC:4 * DC]

    P.add("sc", lambda E: E.activation(SGA, GA_, ACT.Sign),
          waits=["ev_mc2"], ev="sgn_a", sem="p0s")
    P.add("sc", lambda E: E.activation(SGB, GB_, ACT.Sign),
          waits=["ev_mc5"], ev="sgn_b", sem="p0s")

    derive_ops = []
    for g_in, sgn, g_out in ((GA_, SGA, pcol.ap()[:]), (GB_, SGB, gcol.ap()[:])):
        derive_ops += [
            lambda E, g_in=g_in: E.tensor_single_scalar(T1, g_in, -1.0, ALU.mult),
            lambda E, g_in=g_in: E.tensor_tensor(T1, T1, g_in, ALU.max),
            lambda E: E.tensor_single_scalar(T1, T1, 0.25, ALU.max),
            lambda E, sgn=sgn, g_out=g_out: E.tensor_tensor(g_out, sgn, T1, ALU.mult),
        ]
    derive_ops += [
        lambda E: E.tensor_tensor(T0, pcol[:], SA_, ALU.mult),
        lambda E: E.tensor_single_scalar(qcol[:], T0, 2.0, ALU.mult),
        lambda E: E.tensor_scalar(T0, CA_, 1.0, 2.0, ALU.add, ALU.mult),
        lambda E: E.tensor_tensor(pcol[:], pcol[:], T0, ALU.mult),
        lambda E: E.tensor_single_scalar(cb1col[:], CB_, 1.0, ALU.add),
        lambda E: E.tensor_tensor(bbcol[:], gcol[:], bbasecol[:], ALU.mult),
        lambda E: E.tensor_reduce(qs[:], qcol[:], axis=AX.X, op=ALU.add),
    ]
    vchain(derive_ops, [f"ev_mc{g}" for g in range(6)] + ["ev_bb", "sgn_a", "sgn_b"],
           "derived", "p0v")

    P.add("t", lambda E: E.matmul(ps[0:1, 7, 48:49], qs[:], onescol[:], start=True, stop=True),
          waits=["derived", "ms_ones"], ev="mm_qb", sem="p0t")
    P.add("sc", lambda E: E.activation(qbar1[:], ps[0:1, 7, 48:49], ACT.Identity),
          waits=["mm_qb"], ev="ev_qb1", sem="p0s")
    P.add("v", lambda E: E.tensor_single_scalar(qbar1[:], qbar1[:], 1.0 / D, ALU.mult),
          waits=["ev_qb1"], ev="sc_qb", sem="p0v")
    P.add("t", lambda E: E.matmul(ps[:, 7, 56:57], ones_sb[:], qbar1[:], start=True, stop=True),
          waits=["sc_qb"], ev="mm_qbb", sem="p0t")
    P.add("sc", lambda E: E.activation(qbar_col[:], ps[:, 7, 56:57], ACT.Identity),
          waits=["mm_qbb"], ev="ev_qbar", sem="p0s")

    def build_mat(tag, col, slot, rowslot, extra_waits):
        P.add("t", lambda E: E.transpose(ps[0:32, 6, 0:128], col, eye_sb[:]),
              waits=extra_waits, ev=f"trm_{tag}", sem="p0t")
        P.add("sc", lambda E: E.activation(silu32[:], ps[0:32, 6, 0:128], ACT.Identity),
              waits=[f"trm_{tag}"], ev=f"evm_{tag}", sem="p0s")
        P.add("sc", lambda E: E.dma_start(out=rowd[rowslot, :].rearrange("(a k) -> a k", a=32),
                                          in_=silu32[:]),
              waits=[f"evm_{tag}"], ev=f"rw_{tag}", sem="rw", amount=16)
        for ch in range(8):
            rs = rows_sb.ap()[:, ch % 2, :]
            waits = [f"rw_{tag}"] if ch < 2 else [f"mmb_{tag}_{ch - 2}"]
            P.add("sy", lambda E, ch=ch, rs=rs: E.dma_start(
                out=rs, in_=rowd[rowslot, 512 * ch:512 * (ch + 1)].rearrange("(p n) -> p n", p=1)),
                waits=waits, ev=f"rl_{tag}_{ch}", sem="ld", amount=16)
            bank = 6 + ch % 2
            P.add("t", lambda E, ch=ch, rs=rs, bank=bank: E.matmul(
                ps[:, bank, :], ones_sb[:], rs, start=True, stop=True),
                waits=[f"rl_{tag}_{ch}"] + ([f"evb_{tag}_{ch - 2}"] if ch >= 2 else []),
                ev=f"mmb_{tag}_{ch}", sem="p0t")
            P.add("sc", lambda E, ch=ch, slot=slot, bank=bank: E.activation(
                matbuf.ap()[:, slot, 512 * ch:512 * (ch + 1)], ps[:, bank, :], ACT.Identity),
                waits=[f"mmb_{tag}_{ch}"], ev=f"evb_{tag}_{ch}", sem="p0s")

    build_mat("P", pcol[:], 0, 0, ["derived", "ev_qbar", f"mv{DC - 1}", "md5"])
    build_mat("Q", qcol[:], 1, 1, [f"evb_P_7", f"mv{DC - 1}", "md5"])

    NVA_PER_TILE = 4
    NSA_PER_TILE = 4
    for i in range(NT):
        xs = xbuf[i % 2]
        waits = []
        if i >= 2:
            waits.append(f"pa_x1_{i - 2}")
        P.add("sy", lambda E, i=i, xs=xs: E.dma_start(out=xs, in_=x_in[128 * i:128 * (i + 1), :]),
              waits=waits, ev=f"xa{i}", sem="xa", amount=16)
        P.add("v", lambda E, i=i, xs=xs: E.tensor_reduce(st(i, "s0"), xs, axis=AX.X, op=ALU.add),
              waits=[f"xa{i}"] + ([f"sa_u1_{i - 2}"] if i >= 2 else []),
              ev=f"pa_s0_{i}", sem="pa")
        P.add("sc", lambda E, i=i, xs=xs: E.activation(
            junkbuf[:], xs, ACT.Square, accum_out=st(i, "q0")),
            waits=[f"xa{i}"], ev=f"sa_q0_{i}", sem="sa")
        vchain([
            lambda E, i=i: E.tensor_single_scalar(st(i, "m0"), st(i, "s0"), 1.0 / D, ALU.mult),
            lambda E, i=i: E.tensor_tensor(st(i, "t0"), st(i, "m0"), st(i, "m0"), ALU.mult),
            lambda E, i=i: E.tensor_single_scalar(st(i, "var"), st(i, "q0"), 1.0 / D, ALU.mult),
            lambda E, i=i: E.tensor_tensor(st(i, "var"), st(i, "var"), st(i, "t0"), ALU.subtract),
        ], [f"pa_s0_{i}", f"sa_q0_{i}"] + (["ms_eps"] if i == 0 else []), f"va_s0_{i}", "va")
        P.add("sc", lambda E, i=i: E.activation(st(i, "t1"), st(i, "var"), ACT.Sqrt, bias=epscol[:]),
              waits=[f"va_s0_{i}"], ev=f"sa_sd0_{i}", sem="sa")
        vchain([
            lambda E, i=i: E.reciprocal(st(i, "r0"), st(i, "t1")),
            lambda E, i=i: E.scalar_tensor_tensor(st(i, "nm0"), st(i, "m0"), -1.0, st(i, "r0"),
                                                  ALU.mult, ALU.mult),
        ], [f"sa_sd0_{i}"], f"va_r0_{i}", "va")
        P.add("sc", lambda E, i=i, xs=xs: E.activation(
            u0buf, xs, ACT.Identity, bias=st(i, "nm0"), scale=st(i, "r0")),
            waits=[f"va_r0_{i}"] + ([f"ud1_{i - 1}"] if i else []),
            ev=f"sa_u0_{i}", sem="sa")
        P.add("v", lambda E, i=i: E.scalar_tensor_tensor(
            x1buf, u0buf, 1.0, Pm, ALU.mult, ALU.mult, accum_out=st(i, "dp")),
            waits=[f"sa_u0_{i}", "evb_P_7"] + ([f"ud2_{i - 1}"] if i else []),
            ev=f"va_t_{i}", sem="va")
        P.add("g", lambda E, i=i, xs=xs: E.tensor_tensor(x1buf, x1buf, xs, ALU.add),
              waits=[f"va_t_{i}", "evb_Q_7"], ev=f"pa_x1a_{i}", sem="pa")
        P.add("g", lambda E, i=i: E.tensor_tensor(x1buf, x1buf, Qm, ALU.add),
              ev=f"pa_x1_{i}", sem="pa")
        P.add("sc", lambda E, i=i: E.activation(
            junkbuf[:], x1buf, ACT.Square, accum_out=st(i, "q1")),
            waits=[f"pa_x1_{i}"], ev=f"sa_q1_{i}", sem="sa")
        vchain([
            lambda E, i=i: E.tensor_single_scalar(st(i, "m1"), st(i, "dp"), 1.0 / D, ALU.mult),
            lambda E, i=i: E.tensor_tensor(st(i, "m1"), st(i, "m1"), st(i, "m0"), ALU.add),
            lambda E, i=i: E.tensor_tensor(st(i, "m1"), st(i, "m1"), qbar_col[:], ALU.add),
            lambda E, i=i: E.tensor_tensor(st(i, "t0"), st(i, "m1"), st(i, "m1"), ALU.mult),
            lambda E, i=i: E.tensor_single_scalar(st(i, "var1"), st(i, "q1"), 1.0 / D, ALU.mult),
            lambda E, i=i: E.tensor_tensor(st(i, "var1"), st(i, "var1"), st(i, "t0"), ALU.subtract),
        ], [f"sa_q1_{i}", "ev_qbar"], f"va_s1_{i}", "va")
        P.add("sc", lambda E, i=i: E.activation(st(i, "t1"), st(i, "var1"), ACT.Sqrt, bias=epscol[:]),
              waits=[f"va_s1_{i}"], ev=f"sa_sd1_{i}", sem="sa")
        vchain([
            lambda E, i=i: E.reciprocal(st(i, "r1"), st(i, "t1")),
            lambda E, i=i: E.scalar_tensor_tensor(st(i, "nm1"), st(i, "m1"), -1.0, st(i, "r1"),
                                                  ALU.mult, ALU.mult),
        ], [f"sa_sd1_{i}"], f"va_r1_{i}", "va")
        P.add("sc", lambda E, i=i: E.activation(
            u1buf, x1buf, ACT.Identity, bias=st(i, "nm1"), scale=st(i, "r1")),
            waits=[f"va_r1_{i}"], ev=f"sa_u1_{i}", sem="sa")
        P.add("sc", lambda E, i=i: E.dma_start(out=u1_d[128 * i:128 * (i + 1), :], in_=u1buf),
              waits=[f"sa_u1_{i}"], ev=f"ud1_{i}", sem="ud1", amount=16)
        P.add("sc", lambda E, i=i: E.dma_start(out=x1_d[128 * i:128 * (i + 1), :], in_=x1buf),
              ev=f"ud2_{i}", sem="ud2", amount=16)

    build_mat("G", gcol[:], 0, 2, [f"va_t_{NT - 1}", f"pa_x1_{NT - 1}"])
    build_mat("BB", bbcol[:], 1, 3, ["evb_G_7"])

    phaseA_done = [f"ud1_{NT - 1}", f"ud2_{NT - 1}", f"sa_u1_{NT - 1}"]
    nmm = 0
    nep = 0
    for H in range(2):
        hh = f"h{H}"
        for dc in range(DC):
            for grp in range(2):
                waits = []
                ti = H * 64 + 2 * dc + grp
                if ti >= 4:
                    pi = ti - 4
                    pH, pr = divmod(pi, 64)
                    waits.append(f"tr_h{pH}_{pr // 2}_{pr % 2}")
                if dc == 0 and grp == 0:
                    waits += phaseA_done
                    if H == 1:
                        waits.append(f"mm_h0_oc7_moe7")
                P.add("sy", lambda E, H=H, dc=dc, grp=grp: E.dma_start_transpose(
                    out=hbT[:, dc, 512 * grp:512 * (grp + 1)],
                    in_=u1_d[HTOK * H + 512 * grp:HTOK * H + 512 * (grp + 1),
                             128 * dc:128 * (dc + 1)]),
                    waits=waits, ev=f"tr_{hh}_{dc}_{grp}", sem="tr", amount=16)
            P.add("sc", lambda E, dc=dc: E.activation(
                hbT[:, dc, :], hbT[:, dc, :], ACT.Identity,
                bias=modcol.ap()[:, 3, dc:dc + 1], scale=cb1col[:, dc:dc + 1]),
                waits=[f"tr_{hh}_{dc}_0", f"tr_{hh}_{dc}_1"] +
                      (["evb_BB_7"] if dc == 0 and H == 0 else []),
                ev=f"hb_{hh}_{dc}", sem="hb")
        hb_all = [f"hb_{hh}_{dc}" for dc in range(DC)]

        for n in range(2):
            gg = f"{hh}_{n}"
            def glog_mm(E, H=H, n=n):
                inst = None
                for dc in range(DC):
                    inst = E.matmul(ps[0:9, 0, :], wg9_sb.ap()[:, dc, :],
                                    hbT[:, dc, 512 * n:512 * (n + 1)],
                                    start=(dc == 0), stop=(dc == DC - 1),
                                    skip_group_check=True)
                return inst
            idx = 2 * H + n
            prev_gg = f"h{(idx - 1) // 2}_{(idx - 1) % 2}" if idx >= 1 else None
            waits = hb_all + [f"ld_wg9_{q}" for q in range(4)]
            if prev_gg:
                waits.append(f"gs_glog_{prev_gg}")
            if H == 1 and n == 0:
                waits.append("ep_e1_h0_7_0")
            P.add("t", glog_mm, waits=waits, ev=f"gt_glog_{gg}", sem="gt")
            P.add("sc", lambda E: E.activation(glog_sb[:], ps[0:9, 0, :], ACT.Identity),
                  waits=[f"gt_glog_{gg}"], ev=f"gs_glog_{gg}", sem="gs")
            def her_mm(E, H=H, n=n):
                inst = None
                for dc in range(DC):
                    inst = E.matmul(ps[:, 1, :], acat_sb.ap()[:, dc, :],
                                    hbT[:, dc, 512 * n:512 * (n + 1)],
                                    start=(dc == 0), stop=(dc == DC - 1),
                                    skip_group_check=True)
                return inst
            her_waits = [f"ld_acat_{q}" for q in range(4)]
            if prev_gg:
                her_waits.append(f"gs_her_{prev_gg}")
            if H == 1:
                her_waits.append(f"gv_hw_h0_{n}")
                if n == 0:
                    her_waits.append("ep_e1_h0_7_1")
            P.add("t", her_mm, waits=her_waits, ev=f"gt_her_{gg}", sem="gt")
            P.add("sc", lambda E, n=n: E.activation(her_sb.ap()[:, n, :], ps[:, 1, :], ACT.Identity),
                  waits=[f"gt_her_{gg}"], ev=f"gs_her_{gg}", sem="gs")
            for j in range(4):
                jj = 4 * n + j
                sl = 16 * (jj % 2)
                g9w = [f"gs_glog_{gg}"]
                if jj >= 2:
                    g9w.append(f"gs_g9_{hh}_{jj - 2}")
                elif H == 1:
                    g9w.append(f"gv_w_h0_{jj + 6}")
                    if jj == 0:
                        g9w.append("ep_e1_h0_7_2")
                P.add("t", lambda E, j=j, sl=sl: E.transpose(
                    ps[:, 2, sl:sl + 9], glog_sb[0:9, 128 * j:128 * (j + 1)], eye_sb[0:9, 0:9]),
                    waits=g9w, ev=f"gt_g9_{hh}_{jj}", sem="gt")
                P.add("sc", lambda E, jj=jj, sl=sl: E.activation(
                    g9.ap()[:, jj, :], ps[:, 2, sl:sl + 9], ACT.Identity),
                    waits=[f"gt_g9_{hh}_{jj}"], ev=f"gs_g9_{hh}_{jj}", sem="gs")
                def gate1(E, jj=jj):
                    return E.reduce_max(gst.ap()[:, jj, 0:1], g9.ap()[:, jj, 0:8],
                                        axis=AX.X, negate=True)
                P.add("v", gate1, waits=[f"gs_g9_{hh}_{jj}"], ev=f"gv_mx_{hh}_{jj}", sem="gv")
                P.add("sc", lambda E, jj=jj: E.activation(
                    g9.ap()[:, jj, 0:8], g9.ap()[:, jj, 0:8], ACT.Exp, bias=gst.ap()[:, jj, 0:1]),
                    waits=[f"gv_mx_{hh}_{jj}"], ev=f"gs_exp_{hh}_{jj}", sem="gs")
                P.add("sc", lambda E, jj=jj: E.activation(
                    gst.ap()[:, jj, 3:4], g9.ap()[:, jj, 8:9], ACT.Sigmoid, bias=bthr_col[:]),
                    ev=f"gs_sig_{hh}_{jj}", sem="gs")
                gv = gst.ap()[:, jj, :]
                w8j = w8.ap()[:, jj, :]
                g9j = g9.ap()[:, jj, 0:8]
                vchain([
                    lambda E, gv=gv, g9j=g9j: E.tensor_reduce(gv[:, 1:2], g9j, axis=AX.X, op=ALU.add),
                    lambda E, gv=gv: E.reciprocal(gv[:, 2:3], gv[:, 1:2]),
                    lambda E, gv=gv: E.tensor_single_scalar(gv[:, 3:4], gv[:, 3:4], -0.125, ALU.mult),
                    lambda E, gv=gv, g9j=g9j, w8j=w8j: E.tensor_scalar(
                        w8j, g9j, gv[:, 2:3], gv[:, 3:4], ALU.mult, ALU.add),
                    lambda E, w8j=w8j: E.tensor_single_scalar(w8j, w8j, 0.0, ALU.max),
                    lambda E, gv=gv, w8j=w8j: E.tensor_reduce(gv[:, 4:5], w8j, axis=AX.X, op=ALU.add),
                    lambda E, gv=gv: E.tensor_single_scalar(gv[:, 4:5], gv[:, 4:5], 1e-12, ALU.max),
                    lambda E, gv=gv: E.reciprocal(gv[:, 6:7], gv[:, 4:5]),
                    lambda E, gv=gv: E.tensor_single_scalar(gv[:, 6:7], gv[:, 6:7], 2.0, ALU.mult),
                    lambda E, gv=gv, w8j=w8j: E.tensor_scalar(w8j, w8j, gv[:, 6:7], None, ALU.mult),
                ], [f"gs_exp_{hh}_{jj}", f"gs_sig_{hh}_{jj}"], f"gv_w_{hh}_{jj}", "gv")
                P.add("t", lambda E, jj=jj: E.transpose(
                    ps[0:8, 2, 128 + 128 * (jj % 2):256 + 128 * (jj % 2)],
                    w8.ap()[:, jj, :], eye_sb[:]),
                    waits=[f"gv_w_{hh}_{jj}"] + ([f"gs_wT_{hh}_{jj - 2}"] if jj >= 2 else
                                                 ([f"gs_wT_h0_{jj + 6}"] if H == 1 else [])),
                    ev=f"gt_wT_{hh}_{jj}", sem="gt")
                P.add("sc", lambda E, jj=jj: E.activation(
                    wT_sb[0:8, 128 * jj:128 * (jj + 1)],
                    ps[0:8, 2, 128 + 128 * (jj % 2):256 + 128 * (jj % 2)], ACT.Identity),
                    waits=[f"gt_wT_{hh}_{jj}"], ev=f"gs_wT_{hh}_{jj}", sem="gs")
            wexp_waits = [f"gs_wT_{hh}_{4 * n + j}" for j in range(4)] + ["ld_e16"]
            if prev_gg:
                wexp_waits.append(f"gv_hw_{prev_gg}")
            if H == 1 and n == 0:
                wexp_waits.append("ep_e1_h0_7_3")
            P.add("t", lambda E, n=n: E.matmul(
                ps[:, 3, :], e16_sb[:], wT_sb[0:8, 512 * n:512 * (n + 1)],
                start=True, stop=True),
                waits=wexp_waits, ev=f"gt_wexp_{gg}", sem="gt")
            P.add("v", lambda E, n=n: E.scalar_tensor_tensor(
                hw_sb.ap()[:, 512 * n:512 * (n + 1)], her_sb.ap()[:, n, :], 1.0,
                ps[:, 3, :], ALU.mult, ALU.mult),
                waits=[f"gt_wexp_{gg}", f"gs_her_{gg}"], ev=f"gv_hw_{gg}", sem="gv")

        for oc in range(8):
            for dc in range(DC):
                gidx = (H * 8 + oc) * DC + dc
                slot = gidx % 4
                waits = []
                if gidx >= 4:
                    pd, pc_ = divmod(gidx - 4, DC)
                    waits.append(f"mm_h{pd // 8}_oc{pd % 8}_dc{pc_}")
                P.add("g", lambda E, H=H, oc=oc, dc=dc, slot=slot: E.dma_start(
                    out=wring.ap()[:, slot, :],
                    in_=wbT[128 * dc:128 * (dc + 1), 512 * oc:512 * (oc + 1)]),
                    waits=waits, ev=f"wl_{hh}_{oc}_{dc}", sem="wl", amount=16)
            for dc in range(DC):
                gidx = (H * 8 + oc) * DC + dc
                slot = gidx % 4
                if dc == 0:
                    for j in range(8):
                        waits = [f"wl_{hh}_{oc}_0"]
                        if j == 0:
                            waits += [f"gv_hw_{hh}_0", f"gv_hw_{hh}_1"]
                            if oc == 0 and H == 0:
                                waits += ["evb_G_7", "evb_BB_7"]
                        if oc > 0:
                            waits.append(f"ep_e1_{hh}_{oc - 1}_{j}")
                        elif H == 1:
                            waits.append(f"ep_e1_h0_7_{j}")
                        P.add("t", lambda E, slot=slot, j=j: E.matmul(
                            ps[:, j, :], hbT[:, 0, 128 * j:128 * (j + 1)],
                            wring.ap()[:, slot, :], start=True, stop=False,
                            skip_group_check=True),
                            waits=waits,
                            ev=(f"mm_{hh}_oc{oc}_dc0" if j == 7 else f"mm_{hh}_oc{oc}_d0j{j}"),
                            sem="mm")
                    continue
                def base_mm(E, dc=dc, slot=slot):
                    inst = None
                    for j in range(8):
                        inst = E.matmul(ps[:, j, :], hbT[:, dc, 128 * j:128 * (j + 1)],
                                        wring.ap()[:, slot, :], start=False, stop=False,
                                        skip_group_check=True)
                    return inst
                P.add("t", base_mm, waits=[f"wl_{hh}_{oc}_{dc}"],
                      ev=f"mm_{hh}_oc{oc}_dc{dc}", sem="mm")
            for j in range(8):
                P.add("t", lambda E, H=H, oc=oc, j=j: E.matmul(
                    ps[:, j, :], hw_sb.ap()[:, 128 * j:128 * (j + 1)],
                    bcat_sb.ap()[:, 512 * oc:512 * (oc + 1)],
                    start=False, stop=True, skip_group_check=True),
                    waits=([f"mm_{hh}_oc{oc}_dc{DC - 1}", "ld_bcat"] if j == 0 else []),
                    ev=f"mm_{hh}_oc{oc}_moe{j}", sem="mm")
                k = (H * 8 + oc) * 8 + j
                r3 = k % 3
                P.add("sc", lambda E, H=H, oc=oc, j=j, r3=r3: E.dma_start(
                    out=x1r.ap()[:, r3, :],
                    in_=x1_d[HTOK * H + 128 * j:HTOK * H + 128 * (j + 1),
                             512 * oc:512 * (oc + 1)]),
                    waits=([f"ep_y_h{(k - 3) // 64}_{((k - 3) // 8) % 8}_{(k - 3) % 8}"] if k >= 3 else []),
                    ev=f"xr_{hh}_{oc}_{j}", sem="xr", amount=16)
                P.add("v", lambda E, oc=oc, j=j, r3=r3, H=H: E.scalar_tensor_tensor(
                    e1r.ap()[:, r3, :], Gm[:, 512 * oc:512 * (oc + 1)],
                    1.0, ps[:, j, :], ALU.mult, ALU.mult),
                    waits=[f"mm_{hh}_oc{oc}_moe{j}"] +
                          ([f"ep_p_h{(k - 3) // 64}_{((k - 3) // 8) % 8}_{(k - 3) % 8}"] if k >= 3 else []) +
                          (["evb_G_7", "evb_BB_7"] if k == 0 else []),
                    ev=f"ep_e1_{hh}_{oc}_{j}", sem="epv")
                P.add("g", lambda E, oc=oc, r3=r3, H=H: E.tensor_tensor(
                    e1r.ap()[:, r3, :], e1r.ap()[:, r3, :],
                    BBm[:, 512 * oc:512 * (oc + 1)], ALU.add),
                    waits=[f"ep_e1_{hh}_{oc}_{j}"], ev=f"ep_p_{hh}_{oc}_{j}", sem="epp")
                P.add("v", lambda E, r3=r3: E.tensor_tensor(
                    yr.ap()[:, r3, :], e1r.ap()[:, r3, :], x1r.ap()[:, r3, :], ALU.add),
                    waits=[f"ep_p_{hh}_{oc}_{j}", f"xr_{hh}_{oc}_{j}"] +
                          ([f"yd_h{(k - 3) // 64}_{((k - 3) // 8) % 8}_{(k - 3) % 8}"] if k >= 3 else []),
                    ev=f"ep_y_{hh}_{oc}_{j}", sem="epv")
                P.add("sy", lambda E, H=H, oc=oc, j=j, r3=r3: E.dma_start(
                    out=y_out[HTOK * H + 128 * j:HTOK * H + 128 * (j + 1),
                              512 * oc:512 * (oc + 1)],
                    in_=yr.ap()[:, r3, :]),
                    waits=[f"ep_y_{hh}_{oc}_{j}"], ev=f"yd_{hh}_{oc}_{j}", sem="yd", amount=16)

    if DEBUG:
        all_ep = ["ep_y_h1_7_7"]
        cols_list = [scol, bbasecol, pcol, qcol, cb1col, gcol, bbcol, scol]
        P.add("sy", lambda E: E.dma_start(out=dbg["d_modag"].rearrange("(p n) -> p n", p=1),
                                          in_=modag_d.rearrange("(p n) -> p n", p=1)),
              waits=all_ep, ev="dbg0", sem="yd", amount=16)
        for ci, cb in enumerate(cols_list):
            P.add("sy", lambda E, ci=ci, cb=cb: E.dma_start(
                out=dbg["d_cols"][ci], in_=cb[:]),
                ev=f"dbgc{ci}", sem="yd", amount=16)
        P.add("sy", lambda E: E.dma_start(out=dbg["d_stats"], in_=stats[:]), ev="dbg1", sem="yd", amount=16)
        P.add("sy", lambda E: E.dma_start(out=dbg["d_u1"], in_=u1_d[0:128, :]), ev="dbg2", sem="yd", amount=16)
        P.add("sy", lambda E: E.dma_start(out=dbg["d_x1"], in_=x1_d[0:128, :]), ev="dbg3", sem="yd", amount=16)
        P.add("sy", lambda E: E.dma_start(out=dbg["d_hw"], in_=hw_sb[:]), ev="dbg4", sem="yd", amount=16)
        P.add("sy", lambda E: E.dma_start(out=dbg["d_g9"], in_=g9[:]), ev="dbg5", sem="yd", amount=16)
        P.add("sy", lambda E: E.dma_start(out=dbg["d_w8"], in_=w8[:]), ev="dbg6", sem="yd", amount=16)
        P.add("sy", lambda E: E.dma_start(out=dbg["d_wt"], in_=wT_sb[:]), ev="dbg7", sem="yd", amount=16)
        P.add("sy", lambda E: E.dma_start(out=dbg["d_qbar"], in_=qbar_col[:]), ev="dbg8", sem="yd", amount=16)
        P.add("sy", lambda E: E.dma_start(out=dbg["d_matg"], in_=Gm), ev="dbg9", sem="yd", amount=16)
        P.add("sy", lambda E: E.dma_start(out=dbg["d_matbb"], in_=BBm), ev="dbg10", sem="yd", amount=16)
        P.add("sy", lambda E: E.dma_start(out=dbg["d_modsl"].rearrange("(p n) -> p n", p=1),
                                          in_=modsl_d.rearrange("(p n) -> p n", p=1)),
              waits=["md5"], ev="dbg11", sem="yd", amount=16)
        P.add("sy", lambda E: E.dma_start(out=dbg["d_scolb"], in_=scolb[:]),
              waits=["ev_scolb"], ev="dbg12", sem="yd", amount=16)

    def final_wait(E):
        return E.engine_nop()
    fw = [f"yd_h{H}_{oc}_{j}" for H in range(2) for oc in range(8) for j in range(8)]
    if DEBUG:
        fw += ["dbg0", "dbg1", "dbg2", "dbg3", "dbg4", "dbg5", "dbg6", "dbg7",
               "dbg8", "dbg9", "dbg10", "dbg11", "dbg12"] + [f"dbgc{ci}" for ci in range(8)]
    P.add("v", final_wait, waits=fw)

    with nc.Block() as blk:
        @blk.sync
        def _(E):
            P.emit("sy", E, sems)

        @blk.gpsimd
        def _(E):
            P.emit("g", E, sems)

        @blk.scalar
        def _(E):
            P.emit("sc", E, sems)

        @blk.vector
        def _(E):
            P.emit("v", E, sems)

        @blk.tensor
        def _(E):
            P.emit("t", E, sems)

    ctx.close()
    return nc


_NC_CACHE = None


def kernel(L0_attn_x, L0_ffn_W_base, L0_ffn_b_base, L0_ffn_W_gate, L0_ffn_W_thr,
           L0_ffn_b_thr, L0_ffn_A_w, L0_ffn_B_w, L0_c, L0_wmod):
    global _NC_CACHE
    x = np.ascontiguousarray(np.asarray(L0_attn_x, dtype=np.float32).reshape(16384, D))
    wbT = np.ascontiguousarray(np.asarray(L0_ffn_W_base, dtype=np.float32).T)
    wg9t = np.ascontiguousarray(np.concatenate(
        [np.asarray(L0_ffn_W_gate, dtype=np.float32),
         np.asarray(L0_ffn_W_thr, dtype=np.float32)[None, :]], axis=0).T)
    acatT = np.ascontiguousarray(np.asarray(L0_ffn_A_w, dtype=np.float32).reshape(128, D).T)
    bcat = np.ascontiguousarray(np.asarray(L0_ffn_B_w, dtype=np.float32)
                                .transpose(0, 2, 1).reshape(128, D))
    c = np.ascontiguousarray(np.asarray(L0_c, dtype=np.float32).reshape(D))
    bb = np.ascontiguousarray(np.asarray(L0_ffn_b_base, dtype=np.float32).reshape(D))
    bthr = np.ascontiguousarray(np.asarray(L0_ffn_b_thr, dtype=np.float32).reshape(1, 1))
    wmod = np.asarray(L0_wmod, dtype=np.float32)
    eye = np.eye(128, dtype=np.float32)
    e16 = np.zeros((8, 128), dtype=np.float32)
    for ee in range(8):
        e16[ee, 16 * ee:16 * (ee + 1)] = 1.0
    ones = np.ones((1, 128), dtype=np.float32)

    if _NC_CACHE is None:
        _NC_CACHE = build()
    nc = _NC_CACHE

    in_maps = []
    for r in range(NCORES):
        in_maps.append(dict(
            x_in=np.ascontiguousarray(x[TOK * r:TOK * (r + 1)]),
            wmod_sl=np.ascontiguousarray(wmod[:, MODC * r:MODC * (r + 1)]),
            wbT=wbT, wg9t=wg9t, acatT=acatT, bcat_in=bcat,
            c_in=c, bb_in=bb, bthr_in=bthr,
            eye_in=eye, e16_in=e16, ones_in=ones,
        ))
    res = run_bass_kernel_spmd(nc, in_maps, core_ids=list(range(NCORES)))
    y = np.concatenate([np.asarray(res.results[r]["y_out"]) for r in range(NCORES)], axis=0)
    return y.reshape(1, 16384, D).astype(np.float32)
